# version 13
# speedup vs baseline: 1.4169x; 1.0097x over previous
import numpy as np
import ml_dtypes
import concourse.bass as bass
import concourse.mybir as mybir
from concourse.bass_utils import run_bass_kernel_spmd

F32 = mybir.dt.float32
BF16 = mybir.dt.bfloat16
AF = mybir.ActivationFunctionType
ALU = mybir.AluOpType
AX = mybir.AxisListType
BF = np.dtype(ml_dtypes.bfloat16)

NCORES = 8


class Sem:
    def __init__(self, handle, name):
        self.h = handle
        self.name = name
        self.n = 0


class Build:
    ENGS = ("sync", "gpsimd", "tensor", "vector", "scalar")

    def __init__(self, nc):
        self.nc = nc
        self.ops = {e: [] for e in self.ENGS}
        self._ctx = []
        self._scopes = []
        self._uid = 0
        self._dirty = {}
        self._dread = {}

    def sem(self, name):
        self._uid += 1
        cm = self.nc.semaphore(f"{name}_u{self._uid}")
        h = cm.__enter__()
        self._ctx.append(cm)
        return Sem(h, name)

    def sb(self, name, shape, dt):
        self._uid += 1
        cm = self.nc.sbuf_tensor(f"{name}_u{self._uid}", shape, dt)
        t = cm.__enter__()
        (self._scopes[-1] if self._scopes else self._ctx).append(cm)
        return t

    def ps(self, name, shape, dt):
        self._uid += 1
        cm = self.nc.psum_tensor(f"{name}_u{self._uid}", shape, dt)
        t = cm.__enter__()
        (self._scopes[-1] if self._scopes else self._ctx).append(cm)
        return t

    def push(self):
        self._scopes.append([])

    def pop(self):
        for cm in reversed(self._scopes.pop()):
            cm.__exit__(None, None, None)

    def close(self):
        for cm in reversed(self._ctx):
            cm.__exit__(None, None, None)

    def wait(self, eng, sem, val):
        if val > 0:
            self.ops[eng].append(lambda e, s=sem.h, v=val: e.wait_ge(s, v))

    READ_KEYS = ("in_", "in0", "in1", "scalar1", "scalar2", "bias", "scale",
                 "alpha", "identity", "lhsT", "rhs")

    def _apname(self, v):
        try:
            return v.name
        except Exception:
            return None

    def op(self, eng, method, *args, inc=None, inc_n=1, **kwargs):
        if eng in ("vector", "scalar"):
            dirty = self._dirty.setdefault(eng, set())
            dread = self._dread.setdefault(eng, set())
            rnames = set()
            for k in self.READ_KEYS:
                n = self._apname(kwargs.get(k))
                if n:
                    rnames.add(n)
            out = kwargs.get("out")
            if method == "memset" and args:
                out = args[0]
            wnames = set()
            for o in (out, kwargs.get("accum_out")):
                n = self._apname(o)
                if n:
                    wnames.add(n)
            if (rnames & dirty) or (wnames & (dirty | dread)):
                self.ops[eng].append(lambda e: e.drain())
                dirty.clear()
                dread.clear()
            dirty.update(wnames)
            dread.update(rnames)
        if inc is not None:
            inc.n += inc_n
            def f(e, m=method, a=args, kw=kwargs, s=inc.h, n=inc_n):
                getattr(e, m)(*a, **kw).then_inc(s, n)
        else:
            def f(e, m=method, a=args, kw=kwargs):
                getattr(e, m)(*a, **kw)
        self.ops[eng].append(f)

    def dma(self, eng, out, in_, inc=None):
        self.op(eng, "dma_start", out=out, in_=in_, inc=inc, inc_n=16)

    def dma_pool_init(self, n=16):
        self._dmapool = {q: [self.sem(f"dmp_{q}{i}") for i in range(n)]
                         for q in ("sync", "gpsimd")}
        self._dmapool_i = {"sync": 0, "gpsimd": 0}

    def dma_batch_sem(self, queue="sync"):
        pool = self._dmapool[queue]
        s = pool[self._dmapool_i[queue] % len(pool)]
        self._dmapool_i[queue] += 1
        return s

    def drain(self, eng):
        if eng in self._dirty:
            self._dirty[eng].clear()
        if eng in self._dread:
            self._dread[eng].clear()
        self.ops[eng].append(lambda e: e.drain())

    def bump(self, eng, sem):
        sem.n += 1
        if eng in self._dirty:
            self._dirty[eng].clear()
        if eng in self._dread:
            self._dread[eng].clear()
        self.ops[eng].append(lambda e, s=sem.h: e.drain().then_inc(s, 1))


def prep_host(inputs):
    f = np.ascontiguousarray
    w = {}
    w["gateT"] = f(inputs["L0_ffn_gate"].T)
    w["cb"] = f(inputs["L0_ffn_corr_bias"].reshape(1, 8))
    w["fc1T"] = f(inputs["L0_ffn_fc1_w"].T.reshape(8, 128, 512)
                  .transpose(1, 0, 2).reshape(128, 4096))
    w["fc2T"] = f(inputs["L0_ffn_fc2_w"].T.reshape(4, 128, 1024)
                  .transpose(1, 0, 2).reshape(128, 4096))
    w["w1T"] = f(inputs["L0_ffn_w1"].transpose(0, 2, 1).reshape(8, 4, 128, 512)
                 .transpose(0, 2, 1, 3).reshape(8, 128, 2048))
    w["w2T"] = f(inputs["L0_ffn_w2"].transpose(0, 2, 1).reshape(8, 4, 128, 512)
                 .transpose(0, 2, 1, 3).reshape(8, 128, 2048))
    w["upT"] = f(inputs["L0_ffn_up_w"].T.reshape(8, 128, 32, 128)
                 .transpose(2, 1, 0, 3).reshape(32, 128, 1024))
    w["downT"] = f(inputs["L0_ffn_down_w"].T.reshape(2, 16, 128, 8, 128)
                   .transpose(3, 0, 2, 1, 4).reshape(8, 2, 128, 2048))
    for nm, key in (("wqT", "L0_attn_W_Q"), ("wkT", "L0_attn_W_K")):
        m = inputs[key].transpose(2, 0, 1).reshape(1024, 1024)
        w[nm] = f(m.reshape(8, 128, 8, 128).transpose(2, 1, 0, 3).reshape(8, 128, 1024))
    w["wvTk0"] = f(inputs["L0_attn_W_V"].transpose(2, 0, 1).reshape(8, 128, 1024))
    w["wv1k"] = f(inputs["L1_attn_Wv"].reshape(8, 128, 1024))
    w["woT2"] = f(inputs["L0_attn_W_O"].T.reshape(2, 4, 128, 8, 128)
                  .transpose(3, 0, 2, 1, 4).reshape(8, 2, 128, 512))
    w["wqk"] = f(inputs["L1_attn_Wqk"].reshape(8, 128, 16, 128)
                 .transpose(2, 1, 0, 3).reshape(16, 128, 1024))
    w["bqk_row"] = f(inputs["L1_attn_bqk"].reshape(1, 2048).astype(BF))
    w["bv_row"] = f(inputs["L1_attn_bv"].reshape(1, 1024).astype(BF))
    w["g1p"] = f(inputs["L1_ffn_G1_R"][0].transpose(1, 2, 0).reshape(4, 1024))
    w["g2p"] = f(inputs["L1_ffn_G2_R"][:, :, 0, :].transpose(0, 2, 1).reshape(4, 256))
    w["g1pi"] = f(inputs["L1_ffn_G1_I"][0].transpose(1, 2, 0).reshape(4, 1024))
    w["g2pi"] = f(inputs["L1_ffn_G2_I"][:, :, 0, :].transpose(0, 2, 1).reshape(4, 256))
    w["b_re_row"] = f(inputs["L1_ffn_bias_real"].reshape(1, 512).astype(BF))
    w["b_im_row"] = f(inputs["L1_ffn_bias_imag"].reshape(1, 512).astype(BF))
    w["ones1"] = np.ones((1, 128), dtype=np.float32)
    w["onesP"] = np.ones((128, 128), dtype=np.float32)
    w["onesrow_b"] = np.ones((1, 512), dtype=BF)
    w["ones1b"] = np.ones((1, 128), dtype=BF)
    bd2 = np.zeros((128, 2), dtype=BF)
    bd2[0:64, 0] = 1
    bd2[64:128, 1] = 1
    w["bd2"] = bd2
    w["onescol"] = np.ones((128, 1), dtype=BF)
    w["ident8"] = np.eye(8, dtype=np.float32)
    w["ident128"] = np.eye(128, dtype=np.float32)
    pp, qq = np.meshgrid(np.arange(128), np.arange(128), indexing="ij")
    w["trimask"] = (pp <= qq).astype(BF)
    ind = np.zeros((8, 8 * 128), dtype=np.float32)
    for e in range(8):
        ind[e, 128 * e:128 * (e + 1)] = 1.0
    w["ind8x"] = ind
    return w


DRAM_SPECS = [
    ("gateT", [1024, 8], F32), ("cb", [1, 8], F32),
    ("fc1T", [128, 4096], F32), ("fc2T", [128, 4096], F32),
    ("w1T", [8, 128, 2048], F32), ("w2T", [8, 128, 2048], F32),
    ("upT", [32, 128, 1024], F32), ("downT", [8, 2, 128, 2048], F32),
    ("wqT", [8, 128, 1024], F32), ("wkT", [8, 128, 1024], F32),
    ("wvTk0", [8, 128, 1024], F32), ("woT2", [8, 2, 128, 512], F32),
    ("wqk", [16, 128, 1024], F32), ("wv1k", [8, 128, 1024], F32),
    ("bqk_row", [1, 2048], BF16), ("bv_row", [1, 1024], BF16),
    ("onesrow_b", [1, 512], BF16),
    ("g1p", [4, 1024], F32), ("g2p", [4, 256], F32),
    ("g1pi", [4, 1024], F32), ("g2pi", [4, 256], F32),
    ("b_re_row", [1, 512], BF16), ("b_im_row", [1, 512], BF16),
    ("ones1", [1, 128], F32), ("onesP", [128, 128], F32), ("ones1b", [1, 128], BF16),
    ("bd2", [128, 2], BF16), ("onescol", [128, 1], BF16),
    ("ident8", [8, 8], F32), ("ident128", [128, 128], F32), ("ind8x", [8, 1024], F32),
    ("trimask", [128, 128], BF16),
]


class WStream:

    _registry = {}

    def __init__(self, b, name, cols, nslots=2, dt=BF16):
        self.b = b
        self.cols = cols
        self.nslots = nslots
        self.arena = b.sb(f"ws_{name}", [128, cols * nslots], dt)
        st = WStream._registry.get((id(b), name))
        if st is None:
            st = {"sems": [(b.sem(f"ws_{name}_l{i}"), b.sem(f"ws_{name}_f{i}"))
                           for i in range(nslots)],
                  "loads": [0] * nslots, "idx": 0}
            WStream._registry[(id(b), name)] = st
        self.sems = st["sems"]
        self._st = st

    @property
    def loads(self):
        return self._st["loads"]

    @property
    def idx(self):
        return self._st["idx"]

    @idx.setter
    def idx(self, v):
        self._st["idx"] = v

    def load(self, parts):
        b = self.b
        s = self.idx % self.nslots
        ld, fr = self.sems[s]
        if self.loads[s] > 0:
            b.wait("gpsimd", fr, self.loads[s])
        base = s * self.cols
        for (c0, ncols, ap2) in parts:
            b.dma("gpsimd", self.arena[:, base + c0:base + c0 + ncols], ap2, inc=ld)
        self.loads[s] += 1
        self.idx += 1
        return (s, ld, 16 * len(parts) * self.loads[s], fr)

    def wait_use(self, eng, tok):
        self.b.wait(eng, tok[1], tok[2])

    def ap(self, tok):
        s = tok[0]
        return self.arena[:, s * self.cols:(s + 1) * self.cols]

    def free(self, eng, tok, sem):
        b = self.b
        s, ld, target, fr = tok
        b.wait("scalar", ld, target)
        b.wait("scalar", sem, sem.n)
        fr.n += 1
        b.ops["scalar"].append(lambda e, sh=fr.h: e.drain().then_inc(sh, 1))


def build_program(upto="all", taps=()):
    nc = bass.Bass("TRN2", target_bir_lowering=False)
    b = Build(nc)

    dr = {"x": nc.dram_tensor("x", [1024, 2048], F32, kind="ExternalInput").ap()}
    for nm, shp, dt in DRAM_SPECS:
        dr[nm] = nc.dram_tensor(nm, shp, dt, kind="ExternalInput").ap()
    y = nc.dram_tensor("y", [1024, 2048], F32, kind="ExternalOutput").ap()
    tt_scr = nc.dram_tensor("tt_scr", [1024, 256], F32, kind="Internal").ap()
    tap_out = {}
    for tn in taps:
        tap_out[tn] = nc.dram_tensor("tap_" + tn, [1024, 2048], F32, kind="ExternalOutput").ap()

    sT = b.sem("sT"); sV = b.sem("sV"); sS = b.sem("sS")
    sX = b.sem("sX"); sW = b.sem("sW"); sST = b.sem("sST")
    b.dma_pool_init(16)

    X = [b.sb(f"X{j}", [128, 2048], F32) for j in range(8)]
    XN = [b.sb(f"XN{j}", [128, 2048], BF16) for j in range(8)]
    ones1 = b.sb("c_ones1", [1, 128], F32)
    onesP = b.sb("c_onesP", [128, 128], F32)
    ones1b = b.sb("c_ones1b", [1, 128], BF16)
    onescol = b.sb("c_onescol", [128, 1], BF16)
    bd2 = b.sb("c_bd2", [128, 2], BF16)
    ident8 = b.sb("c_id8", [8, 8], F32)
    ident128 = b.sb("c_id128", [128, 128], F32)
    trimask = b.sb("c_tri", [128, 128], BF16)
    ind8x = b.sb("c_ind8x", [8, 1024], F32)
    rowsA = b.sb("rowsA", [128, 2048], F32)
    rows3 = b.sb("rows3", [1, 2048], F32)
    gateT = b.sb("a_gateT", [128, 64], BF16)
    cbrow = b.sb("a_cbrow", [1, 32], F32)
    cb32 = b.sb("a_cb32", [128, 32], F32)
    bqkrow = b.sb("a_bqkrow", [1, 2048], BF16)
    onesrow_b = b.sb("a_onesrowb", [1, 512], BF16)
    bvrow = b.sb("a_bvrow", [1, 1024], BF16)
    brerow = b.sb("a_brerow", [1, 512], BF16)
    bimrow = b.sb("a_bimrow", [1, 512], BF16)
    WTre = b.sb("a_WTre", [128, 2048], BF16)
    WTim = b.sb("a_WTim", [128, 2048], BF16)
    WTimn = b.sb("a_WTimn", [128, 2048], BF16)

    q = [b.ps(f"q{i}", [128, 1024], F32) for i in range(4)]

    def qs(vcol, width):
        t = q[vcol // 1024]
        c = vcol % 1024
        return t[:, c:c + width]

    for j in range(8):
        b.dma("sync", X[j][:], dr["x"][128 * j:128 * (j + 1), :], inc=sX)
    X_LD = sX.n
    for (dst, src) in [
        (ones1[:], dr["ones1"]), (onesP[:], dr["onesP"]), (ones1b[:], dr["ones1b"]), (onescol[:], dr["onescol"]),
        (bd2[:], dr["bd2"]), (ident8[:], dr["ident8"]), (ident128[:], dr["ident128"]),
        (trimask[:], dr["trimask"]), (ind8x[:], dr["ind8x"]), (cbrow[0:1, 0:8], dr["cb"]),
        (bqkrow[:], dr["bqk_row"]), (onesrow_b[:], dr["onesrow_b"]),
        (bvrow[:], dr["bv_row"]), (brerow[:], dr["b_re_row"]), (bimrow[:], dr["b_im_row"]),
        (gateT[:].rearrange("p (c n) -> p c n", n=8), dr["gateT"].rearrange("(c p) n -> p c n", p=128)),
    ]:
        b.dma("gpsimd", dst, src, inc=sW)
    W_LD = sW.n

    b.wait("vector", sW, W_LD)
    b.op("vector", "tensor_copy", out=cbrow[0:1, :].rearrange("p (c e) -> p c e", e=8),
         in_=cbrow[0:1, 0:8].unsqueeze(1).broadcast_to([1, 4, 8]), inc=sV)
    b.wait("tensor", sW, W_LD)
    b.wait("tensor", sX, X_LD)
    b.wait("tensor", sV, sV.n)
    b.op("tensor", "matmul", out=q[3][0:128, 0:32], lhsT=ones1[:], rhs=cbrow[0:1, :],
         start=True, stop=True, inc=sT)
    b.wait("vector", sT, sT.n)
    b.op("vector", "tensor_copy", out=cb32[:], in_=q[3][0:128, 0:32], inc=sV)

    def barrier():
        b.bump("tensor", sT)
        b.bump("vector", sV)
        b.bump("scalar", sS)
        tn, vn, sn = sT.n, sV.n, sS.n
        b.wait("tensor", sV, vn); b.wait("tensor", sS, sn)
        b.wait("vector", sT, tn); b.wait("vector", sS, sn)
        b.wait("scalar", sT, tn); b.wait("scalar", sV, vn)
        b.wait("gpsimd", sT, tn); b.wait("gpsimd", sV, vn); b.wait("gpsimd", sS, sn)

    def build_tt():
        b.push()
        g1sb = b.sb("t_g1", [4, 1024], F32); g2sb = b.sb("t_g2", [4, 256], F32)
        g1isb = b.sb("t_g1i", [4, 1024], F32); g2isb = b.sb("t_g2i", [4, 256], F32)
        ttV = b.sb("t_ttV", [128, 2048], F32)
        ttVT = b.sb("t_ttVT", [128, 2048], F32)
        ttW = b.sb("t_ttW", [128, 2048], F32)
        sWG = b.dma_batch_sem("gpsimd")
        for (dst, src) in [(g1sb[:], dr["g1p"]), (g2sb[:], dr["g2p"]),
                           (g1isb[:], dr["g1pi"]), (g2isb[:], dr["g2pi"])]:
            b.dma("gpsimd", dst, src, inc=sWG)
        b.wait("tensor", sWG, sWG.n)
        prev_done = None
        for mi, (g1, g2) in enumerate([(g1sb, g2sb), (g1isb, g2isb)]):
            for ch in range(2):
                if mi == 1 and ch == 0 and prev_done is not None:
                    b.wait("tensor", prev_done[0], prev_done[1])
                b.wait("tensor", sV, sV.n)
                for ns in range(2):
                    b.op("tensor", "matmul", out=q[0][:, 512 * ns:512 * (ns + 1)],
                         lhsT=g2[0:4, 128 * ch:128 * (ch + 1)],
                         rhs=g1[0:4, 512 * ns:512 * (ns + 1)], start=True, stop=True,
                         inc=sT if ns == 1 else None)
                b.wait("vector", sT, sT.n)
                b.op("vector", "tensor_copy", out=ttV[:, 1024 * ch:1024 * (ch + 1)],
                     in_=q[0][:, 0:1024], inc=sV)
            b.wait("tensor", sV, sV.n)
            for ch in range(2):
                for c8 in range(8):
                    b.op("tensor", "transpose",
                         out=q[1 + ch][:, 128 * c8:128 * (c8 + 1)],
                         in_=ttV[:, 1024 * ch + 128 * c8:1024 * ch + 128 * (c8 + 1)],
                         identity=ident128[:], inc=sT if c8 == 7 else None)
                b.wait("vector", sT, sT.n)
                for c8 in range(8):
                    b.op("vector", "tensor_copy",
                         out=ttVT[:, 256 * c8 + 128 * ch:256 * c8 + 128 * (ch + 1)],
                         in_=q[1 + ch][:, 128 * c8:128 * (c8 + 1)],
                         inc=sV if c8 == 7 else None)
            b.wait("sync", sV, sV.n)
            s1 = b.dma_batch_sem()
            b.dma("sync", tt_scr.rearrange("(c p) n -> p c n", p=128),
                  ttVT[:].rearrange("p (c n) -> p c n", n=256), inc=s1)
            b.wait("sync", s1, s1.n)
            s2 = b.dma_batch_sem()
            shuf = tt_scr.rearrange("(r dl bb) (g e) -> dl r g bb e", r=4, dl=8, bb=32, g=16)
            for dl in range(8):
                for r in range(4):
                    b.dma("sync",
                          ttW[16 * dl:16 * (dl + 1), 512 * r:512 * (r + 1)]
                          .rearrange("p (bb e) -> p bb e", e=16),
                          shuf[dl, r], inc=s2)
            prev_done = (s2, s2.n)
            dstW = WTre if mi == 0 else WTim
            b.wait("vector", s2, s2.n)
            b.op("vector", "tensor_copy", out=dstW[:], in_=ttW[:], inc=sV)
        b.wait("vector", sV, sV.n)
        b.op("vector", "tensor_scalar", out=WTimn[:], in0=WTim[:], scalar1=-1.0,
             scalar2=None, op0=ALU.mult, inc=sV)
        b.pop()

    def layernorm():
        b.wait("tensor", sV, sV.n)
        b.wait("tensor", sS, sS.n)
        b.wait("vector", sT, sT.n)
        b.wait("vector", sS, sS.n)
        b.wait("scalar", sV, sV.n)
        b.wait("scalar", sT, sT.n)
        b.wait("gpsimd", sT, sT.n)
        b.wait("gpsimd", sV, sV.n)
        b.wait("gpsimd", sS, sS.n)
        marks = []
        b.push()
        ln_xb = [b.sb(f"ln_xb{i}", [128, 2048], BF16) for i in range(2)]
        ln_xsq = [b.sb(f"ln_xsq{i}", [128, 2048], BF16) for i in range(2)]
        ln_t32 = b.sb("ln_t32", [128, 1024], F32)
        ln_rows = b.sb("ln_rows", [1, 4096], F32)
        for j in range(8):
            d = j % 2
            if j >= 2:
                b.wait("vector", sT, sT.n)
                b.wait("scalar", sT, sT.n)
            b.op("vector", "tensor_copy", out=ln_xb[d][:], in_=X[j][:], inc=sV)
            b.op("scalar", "activation", out=ln_xsq[d][:], in_=X[j][:], func=AF.Square, inc=sS)
            b.wait("tensor", sV, sV.n)
            b.wait("tensor", sS, sS.n)
            for c in range(4):
                b.op("tensor", "matmul", out=qs(512 * c, 512)[0:1, :], lhsT=onescol[:],
                     rhs=ln_xb[d][:, 512 * c:512 * (c + 1)], start=(j == 0), stop=(j == 7))
                b.op("tensor", "matmul", out=qs(2048 + 512 * c, 512)[0:1, :], lhsT=onescol[:],
                     rhs=ln_xsq[d][:, 512 * c:512 * (c + 1)], start=(j == 0), stop=(j == 7),
                     inc=sT if c == 3 else None)
        b.wait("vector", sT, sT.n)
        for h in range(2):
            b.op("vector", "tensor_scalar", out=ln_rows[0:1, 1024 * h:1024 * (h + 1)],
                 in0=q[h][0:1, :], scalar1=1.0 / 1024, scalar2=None, op0=ALU.mult)
        b.op("vector", "tensor_mul", out=rowsA[0:1, :], in0=ln_rows[0:1, 0:2048],
             in1=ln_rows[0:1, 0:2048])
        for h in range(2):
            b.op("vector", "scalar_tensor_tensor", out=ln_rows[0:1, 2048 + 1024 * h:2048 + 1024 * (h + 1)],
                 in0=q[2 + h][0:1, :], scalar=1.0 / 1024, in1=rowsA[0:1, 1024 * h:1024 * (h + 1)],
                 op0=ALU.mult, op1=ALU.subtract, inc=sV if h == 1 else None)
        b.op("vector", "tensor_scalar", out=ln_rows[0:1, 2048:4096], in0=ln_rows[0:1, 2048:4096],
             scalar1=1e-5, scalar2=None, op0=ALU.add, inc=sV)
        b.wait("scalar", sV, sV.n)
        b.op("scalar", "activation", out=ln_rows[0:1, 2048:4096], in_=ln_rows[0:1, 2048:4096],
             func=AF.Ln)
        b.op("scalar", "activation", out=rows3[0:1, :], in_=ln_rows[0:1, 2048:4096],
             func=AF.Exp, scale=-0.5, inc=sS)
        b.wait("vector", sS, sS.n)
        b.op("vector", "scalar_tensor_tensor", out=rowsA[0:1, :], in0=ln_rows[0:1, 0:2048],
             scalar=-1.0, in1=rows3[0:1, :], op0=ALU.mult, op1=ALU.mult, inc=sV)
        b.wait("tensor", sV, sV.n)
        for h in range(2):
            if h == 1:
                b.wait("tensor", sV, sV.n)
            for c in range(2):
                b.op("tensor", "matmul", out=q[2][:, 512 * c:512 * (c + 1)], lhsT=ones1[:],
                     rhs=rows3[0:1, 1024 * h + 512 * c:1024 * h + 512 * (c + 1)],
                     start=True, stop=True)
                b.op("tensor", "matmul", out=q[3][:, 512 * c:512 * (c + 1)], lhsT=ones1[:],
                     rhs=rowsA[0:1, 1024 * h + 512 * c:1024 * h + 512 * (c + 1)],
                     start=True, stop=True, inc=sT if c == 1 else None)
            b.wait("vector", sT, sT.n)
            for j in range(8):
                b.op("vector", "tensor_tensor", out=ln_t32[:],
                     in0=X[j][:, 1024 * h:1024 * (h + 1)], in1=q[2][:, 0:1024], op=ALU.mult)
                b.op("vector", "tensor_tensor", out=XN[j][:, 1024 * h:1024 * (h + 1)],
                     in0=ln_t32[:], in1=q[3][:, 0:1024], op=ALU.add,
                     inc=sV if j == 7 else None)
            marks.append(sV.n)
        b.pop()
        return marks

    def phase_entry_guards():
        b.wait("gpsimd", sV, sV.n)
        b.wait("gpsimd", sT, sT.n)
        b.wait("gpsimd", sS, sS.n)
        b.drain("vector")
        b.wait("vector", sT, sT.n)
        b.wait("vector", sS, sS.n)
        b.drain("scalar")
        b.wait("scalar", sV, sV.n)
        b.wait("scalar", sT, sT.n)

    def moe_ffn(xn_marks):
        phase_entry_guards()
        b.push()
        st_w1 = WStream(b, "w1", 2048)
        st_w2 = WStream(b, "w2", 2048)
        st_fc1 = WStream(b, "fc1", 4096, nslots=1)
        st_fc2 = WStream(b, "fc2", 4096, nslots=1)
        latb = b.sb("m_latb", [128, 2048], BF16)
        latsc = [b.sb("m_latsc", [128, 2048], BF16) for _ in range(2)]
        h2 = b.sb("m_h2", [128, 2048], BF16)
        rsc = [b.sb("m_rsc", [128, 512], F32) for _ in range(2)]
        combb = b.sb("m_combb", [128, 2048], BF16)
        Pf = b.sb("m_Pf", [8, 512], F32)
        PK = b.sb("m_PK", [128, 32], F32)
        SF = b.sb("m_SF", [128, 32], F32)
        M1 = b.sb("m_M1", [128, 32], F32)
        M2 = b.sb("m_M2", [128, 32], F32)
        TMP = b.sb("m_TMP", [128, 32], F32)
        GS = b.sb("m_GS", [128, 32], F32)
        SC = b.sb("m_SC", [128, 32], F32)
        scf = b.sb("m_scf", [8, 512], F32)
        tok_fc1 = st_fc1.load([(0, 4096, dr["fc1T"])])
        tok_fc2 = st_fc2.load([(0, 4096, dr["fc2T"])])

        def v3(t):
            return t[:].rearrange("p (c e) -> p c e", e=8)

        comb_guard = [False]
        for t4 in range(4):
            tc = slice(512 * t4, 512 * (t4 + 1))
            b.wait("tensor", sV, max(sV.n if t4 > 0 else 0, xn_marks[0 if t4 < 2 else 1]))
            for kc in range(8):
                b.op("tensor", "matmul", out=q[1][0:8, 0:512],
                     lhsT=gateT[:, 8 * kc:8 * (kc + 1)], rhs=XN[kc][:, tc],
                     start=(kc == 0), stop=(kc == 7), inc=sT if kc == 7 else None)
            b.wait("scalar", sT, sT.n)
            b.op("scalar", "activation", out=Pf[:], in_=q[1][0:8, 0:512], func=AF.Sigmoid, inc=sS)
            b.wait("tensor", sS, sS.n)
            for c in range(4):
                b.op("tensor", "transpose", out=q[1][0:128, 512 + 8 * c:512 + 8 * (c + 1)],
                     in_=Pf[:, 128 * c:128 * (c + 1)], identity=ident8[:],
                     inc=sT if c == 3 else None)
            b.wait("vector", sT, sT.n)
            b.op("vector", "tensor_copy", out=PK[:], in_=q[1][0:128, 512:544])
            pk, sf, m1, m2, tmp, gs = v3(PK), v3(SF), v3(M1), v3(M2), v3(TMP), v3(GS)
            b.op("vector", "tensor_tensor", out=SF[:], in0=PK[:], in1=cb32[:], op=ALU.add)
            first = True
            for i in range(4):
                for jj in range(i + 1, 4):
                    b.op("vector", "tensor_tensor", out=tmp[:, :, 0:2],
                         in0=sf[:, :, i:i + 5:4], in1=sf[:, :, jj:jj + 5:4], op=ALU.add)
                    if first:
                        b.op("vector", "tensor_copy", out=gs[:, :, 0:2], in_=tmp[:, :, 0:2])
                        first = False
                    else:
                        b.op("vector", "tensor_tensor", out=gs[:, :, 0:2], in0=gs[:, :, 0:2],
                             in1=tmp[:, :, 0:2], op=ALU.max)
            b.op("vector", "tensor_tensor", out=gs[:, :, 2:3], in0=gs[:, :, 0:1],
                 in1=gs[:, :, 1:2], op=ALU.is_ge)
            b.op("vector", "tensor_scalar", out=gs[:, :, 3:4], in0=gs[:, :, 2:3],
                 scalar1=-1.0, scalar2=1.0, op0=ALU.mult, op1=ALU.add)
            b.op("vector", "tensor_tensor", out=sf[:, :, 0:4], in0=sf[:, :, 0:4],
                 in1=gs[:, :, 2:3].broadcast_to([128, 4, 4]), op=ALU.mult)
            b.op("vector", "tensor_tensor", out=sf[:, :, 4:8], in0=sf[:, :, 4:8],
                 in1=gs[:, :, 3:4].broadcast_to([128, 4, 4]), op=ALU.mult)
            b.op("vector", "tensor_tensor", out=tmp[:, :, 0:4], in0=sf[:, :, 0:4],
                 in1=sf[:, :, 4:8], op=ALU.max)
            b.op("vector", "tensor_tensor", out=tmp[:, :, 0:2], in0=tmp[:, :, 0:2],
                 in1=tmp[:, :, 2:4], op=ALU.max)
            b.op("vector", "tensor_tensor", out=tmp[:, :, 0:1], in0=tmp[:, :, 0:1],
                 in1=tmp[:, :, 1:2], op=ALU.max)
            b.op("vector", "tensor_tensor", out=m1[:, :, :], in0=sf[:, :, :],
                 in1=tmp[:, :, 0:1].broadcast_to([128, 4, 8]), op=ALU.is_ge)
            b.op("vector", "scalar_tensor_tensor", out=gs[:, :, :], in0=m1[:, :, :],
                 scalar=-1e30, in1=sf[:, :, :], op0=ALU.mult, op1=ALU.add)
            b.op("vector", "tensor_tensor", out=tmp[:, :, 0:4], in0=gs[:, :, 0:4],
                 in1=gs[:, :, 4:8], op=ALU.max)
            b.op("vector", "tensor_tensor", out=tmp[:, :, 0:2], in0=tmp[:, :, 0:2],
                 in1=tmp[:, :, 2:4], op=ALU.max)
            b.op("vector", "tensor_tensor", out=tmp[:, :, 0:1], in0=tmp[:, :, 0:1],
                 in1=tmp[:, :, 1:2], op=ALU.max)
            b.op("vector", "tensor_tensor", out=m2[:, :, :], in0=gs[:, :, :],
                 in1=tmp[:, :, 0:1].broadcast_to([128, 4, 8]), op=ALU.is_ge)
            b.op("vector", "tensor_tensor", out=m1[:, :, :], in0=m1[:, :, :],
                 in1=m2[:, :, :], op=ALU.add)
            b.op("vector", "tensor_tensor", out=m1[:, :, :], in0=m1[:, :, :],
                 in1=pk[:, :, :], op=ALU.mult)
            b.op("vector", "tensor_tensor", out=tmp[:, :, 0:4], in0=m1[:, :, 0:4],
                 in1=m1[:, :, 4:8], op=ALU.add)
            b.op("vector", "tensor_tensor", out=tmp[:, :, 0:2], in0=tmp[:, :, 0:2],
                 in1=tmp[:, :, 2:4], op=ALU.add)
            b.op("vector", "tensor_tensor", out=tmp[:, :, 0:1], in0=tmp[:, :, 0:1],
                 in1=tmp[:, :, 1:2], op=ALU.add)
            b.op("vector", "tensor_scalar", out=tmp[:, :, 0:1], in0=tmp[:, :, 0:1],
                 scalar1=1e-20, scalar2=None, op0=ALU.add)
            b.op("vector", "reciprocal", out=tmp[:, :, 1:2], in_=tmp[:, :, 0:1])
            b.op("vector", "tensor_tensor", out=v3(SC)[:, :, :], in0=m1[:, :, :],
                 in1=tmp[:, :, 1:2].broadcast_to([128, 4, 8]), op=ALU.mult, inc=sV)
            b.wait("scalar", sV, sV.n)
            b.op("scalar", "activation", out=SC[:], in_=SC[:], func=AF.Sqrt, inc=sS)
            b.wait("tensor", sS, sS.n)
            for c in range(4):
                b.op("tensor", "transpose", out=q[1][0:8, 512 + 128 * c:512 + 128 * (c + 1)],
                     in_=SC[:, 8 * c:8 * (c + 1)], identity=ident128[:],
                     inc=sT if c == 3 else None)
            b.wait("vector", sT, sT.n)
            b.op("vector", "tensor_copy", out=scf[:], in_=q[1][0:8, 512:1024], inc=sV)
            ROUT_V = sV.n

            st_fc1.wait_use("tensor", tok_fc1)
            for lc in range(4):
                b.wait("tensor", sV, sV.n)
                for kc in range(8):
                    b.op("tensor", "matmul", out=q[0][:, 0:512],
                         lhsT=st_fc1.ap(tok_fc1)[:, 512 * kc + 128 * lc:512 * kc + 128 * (lc + 1)],
                         rhs=XN[kc][:, tc], start=(kc == 0), stop=(kc == 7),
                         inc=sT if kc == 7 else None)
                b.wait("vector", sT, sT.n)
                b.op("vector", "tensor_copy", out=latb[:, 512 * lc:512 * (lc + 1)],
                     in_=q[0][:, 0:512], inc=sV)
            LAT_V = sV.n

            lsc_marks = {}

            def emit_latscale(e):
                b.wait("tensor", sV, ROUT_V)
                b.op("tensor", "matmul", out=q[1][:, 0:512],
                     lhsT=ind8x[0:8, 128 * e:128 * (e + 1)],
                     rhs=scf[0:8, :], start=True, stop=True, inc=sT)
                b.wait("vector", sT, sT.n)
                if ("used", e - 2) in lsc_marks:
                    b.wait("vector", sT, lsc_marks[("used", e - 2)])
                b.wait("vector", sV, LAT_V)
                for lc in range(4):
                    b.op("vector", "tensor_tensor",
                         out=latsc[e % 2][:, 512 * lc:512 * (lc + 1)],
                         in0=latb[:, 512 * lc:512 * (lc + 1)], in1=q[1][:, 0:512],
                         op=ALU.mult, inc=sV if lc == 3 else None)
                lsc_marks[e] = sV.n

            if not comb_guard[0]:
                b.wait("tensor", sV, xn_marks[1])
                comb_guard[0] = True
            emit_latscale(0)
            for e in range(8):
                tok1 = st_w1.load([(0, 2048, dr["w1T"][e])])
                tok2 = st_w2.load([(0, 2048, dr["w2T"][e])])
                st_w1.wait_use("tensor", tok1)
                b.wait("tensor", sV, lsc_marks[e])
                relu_marks = {}
                sq_marks = {}
                for ic in range(4):
                    dst = q[0][:, 0:512] if ic % 2 == 0 else q[0][:, 512:1024]
                    if ic >= 2:
                        b.wait("tensor", sS, relu_marks[ic - 2])
                    for kc in range(4):
                        b.op("tensor", "matmul", out=dst,
                             lhsT=st_w1.ap(tok1)[:, 512 * kc + 128 * ic:512 * kc + 128 * (ic + 1)],
                             rhs=latsc[e % 2][:, 512 * kc:512 * (kc + 1)], start=(kc == 0), stop=(kc == 3),
                             inc=(sT if kc == 3 else None))
                    b.wait("scalar", sT, sT.n)
                    if ic >= 2:
                        b.wait("scalar", sV, sq_marks[ic - 2])
                    rb = rsc[ic % 2]
                    b.op("scalar", "activation", out=rb[:], in_=dst, func=AF.Relu, inc=sS)
                    relu_marks[ic] = sS.n
                    b.wait("vector", sS, sS.n)
                    b.op("vector", "tensor_tensor", out=h2[:, 512 * ic:512 * (ic + 1)],
                         in0=rb[:], in1=rb[:], op=ALU.mult, inc=sV)
                    sq_marks[ic] = sV.n
                st_w1.free("tensor", tok1, sT)
                lsc_marks[("used", e)] = sT.n
                if e + 1 < 8:
                    emit_latscale(e + 1)
                st_w2.wait_use("tensor", tok2)
                for kc in range(4):
                    b.wait("tensor", sV, sq_marks[kc])
                    for lc in range(4):
                        b.op("tensor", "matmul", out=qs(2048 + 512 * lc, 512),
                             lhsT=st_w2.ap(tok2)[:, 512 * kc + 128 * lc:512 * kc + 128 * (lc + 1)],
                             rhs=h2[:, 512 * kc:512 * (kc + 1)],
                             start=(e == 0 and kc == 0), stop=(e == 7 and kc == 3),
                             inc=(sT if (kc == 3 and lc == 3) else None))
                st_w2.free("tensor", tok2, sT)
            b.wait("vector", sT, sT.n)
            for lc in range(4):
                b.op("vector", "tensor_copy", out=combb[:, 512 * lc:512 * (lc + 1)],
                     in_=qs(2048 + 512 * lc, 512), inc=sV if lc == 3 else None)
            st_fc2.wait_use("tensor", tok_fc2)
            b.wait("tensor", sV, sV.n)
            for oc in range(8):
                dst = q[0][:, 0:512] if oc % 2 == 0 else q[0][:, 512:1024]
                if oc >= 2:
                    b.wait("tensor", sV, sV.n - 1)
                for kc in range(4):
                    b.op("tensor", "matmul", out=dst,
                         lhsT=st_fc2.ap(tok_fc2)[:, 1024 * kc + 128 * oc:1024 * kc + 128 * (oc + 1)],
                         rhs=combb[:, 512 * kc:512 * (kc + 1)], start=(kc == 0), stop=(kc == 3),
                         inc=(sT if kc == 3 else None))
                b.wait("vector", sT, sT.n)
                b.op("vector", "scalar_tensor_tensor", out=X[oc][:, tc], in0=dst, scalar=0.5,
                     in1=X[oc][:, tc], op0=ALU.mult, op1=ALU.add, inc=sV)
        st_fc1.free("tensor", tok_fc1, sT)
        st_fc2.free("tensor", tok_fc2, sT)
        b.pop()
        barrier()

        b.push()
        st_up = WStream(b, "up", 1024)
        st_dn = WStream(b, "dn", 2048)
        hsh = b.sb("s_hsh", [128, 32 * 512], BF16)
        rsh = [b.sb("s_rsh", [128, 512], F32) for _ in range(2)]
        for t4 in range(4):
            tc = slice(512 * t4, 512 * (t4 + 1))
            toks_up = [st_up.load([(0, 1024, dr["upT"][s])]) for s in range(2)]
            relu_marks = {}
            sq_marks = {}
            for s in range(32):
                tok = toks_up[s % 2]
                st_up.wait_use("tensor", tok)
                dst = q[0][:, 0:512] if s % 2 == 0 else q[1][:, 0:512]
                if s >= 2:
                    b.wait("tensor", sS, relu_marks[s - 2])
                for kc in range(8):
                    b.op("tensor", "matmul", out=dst,
                         lhsT=st_up.ap(tok)[:, 128 * kc:128 * (kc + 1)],
                         rhs=XN[kc][:, tc], start=(kc == 0), stop=(kc == 7),
                         inc=(sT if kc == 7 else None))
                st_up.free("tensor", tok, sT)
                if s + 2 < 32:
                    toks_up[s % 2] = st_up.load([(0, 1024, dr["upT"][s + 2])])
                b.wait("scalar", sT, sT.n)
                if s >= 2:
                    b.wait("scalar", sV, sq_marks[s - 2])
                rb = rsh[s % 2]
                b.op("scalar", "activation", out=rb[:], in_=dst, func=AF.Relu, inc=sS)
                relu_marks[s] = sS.n
                b.wait("vector", sS, sS.n)
                b.op("vector", "tensor_tensor", out=hsh[:, 512 * s:512 * (s + 1)],
                     in0=rb[:], in1=rb[:], op=ALU.mult, inc=sV)
                sq_marks[s] = sV.n
            b.wait("tensor", sV, sV.n)
            toks_dn = [st_dn.load([(0, 2048, dr["downT"][0, 0])]),
                       st_dn.load([(0, 2048, dr["downT"][0, 1])])]
            for o in range(8):
                dst = q[2][:, 0:512] if o % 2 == 0 else q[3][:, 0:512]
                if o >= 2:
                    b.wait("tensor", sV, sV.n - 1)
                for half in range(2):
                    tok = toks_dn[half]
                    st_dn.wait_use("tensor", tok)
                    for kc in range(16):
                        s = 16 * half + kc
                        b.op("tensor", "matmul", out=dst,
                             lhsT=st_dn.ap(tok)[:, 128 * kc:128 * (kc + 1)],
                             rhs=hsh[:, 512 * s:512 * (s + 1)],
                             start=(s == 0), stop=(s == 31),
                             inc=(sT if s in (15, 31) else None))
                    st_dn.free("tensor", tok, sT)
                    if o + 1 < 8:
                        toks_dn[half] = st_dn.load([(0, 2048, dr["downT"][o + 1, half])])
                b.wait("vector", sT, sT.n)
                b.op("vector", "scalar_tensor_tensor", out=X[o][:, tc], in0=dst,
                     scalar=0.5, in1=X[o][:, tc], op0=ALU.mult, op1=ALU.add, inc=sV)
        b.pop()
        barrier()

    def attention(layer, xn_marks):
        phase_entry_guards()
        b.push()
        st_wv = WStream(b, "wv", 512, nslots=8)
        st_qk = WStream(b, "qk", 2048)
        st_wo = WStream(b, "wo", 512) if layer == 0 else None
        Vt = b.sb("at_Vt", [128, 16 * 520], BF16)
        Qp = b.sb("at_Qp", [128, 2048], BF16)
        Kp = b.sb("at_Kp", [128, 2048], BF16)
        SQ = b.sb("at_SQ", [128, 2048], BF16)
        eslot = [b.sb(f"at_e{i}", [128, 1024], BF16) for i in range(2)]
        negB = b.sb("at_negB", [128, 2], F32)
        zf = [b.sb(f"at_zf{j}", [128, 2048], BF16) for j in range(4)] if layer == 0 else None
        vkey = "wvTk0" if layer == 0 else "wv1k"
        tailm = [(0, 0, 0)]

        for g in range(2):
            toks_v = [st_wv.load([(0, 512, dr[vkey][kc][:, 512 * g:512 * (g + 1)])])
                      for kc in range(8)]
            for tcnk in range(16):
                b.op("vector", "memset",
                     Vt[:, 520 * tcnk:520 * (tcnk + 1)].rearrange("p (h c) -> p h c", c=65)[:, :, 64:65],
                     1.0, inc=sV if tcnk == 15 else None)
            VONES = sV.n
            b.wait("tensor", sV, VONES)
            for kc in range(8):
                st_wv.wait_use("tensor", toks_v[kc])
            for tcnk in range(16):
                tsl = slice(128 * tcnk, 128 * (tcnk + 1))
                dst = q[tcnk % 2][:, 0:512]
                if tcnk == 0:
                    b.wait("tensor", sV, max(xn_marks[0], tailm[0][2]))
                elif tcnk == 8:
                    b.wait("tensor", sV, xn_marks[1])
                if tcnk >= 2:
                    b.wait("tensor", sV, sV.n - 1)
                for kc in range(8):
                    last = (kc == 7 and layer == 0)
                    b.op("tensor", "matmul", out=dst, lhsT=XN[kc][:, tsl],
                         rhs=st_wv.ap(toks_v[kc]), start=(kc == 0), stop=last,
                         inc=(sT if last else None))
                if layer == 1:
                    b.op("tensor", "matmul", out=dst, lhsT=ones1b[:],
                         rhs=bvrow[0:1, 512 * g:512 * (g + 1)], start=False, stop=True, inc=sT)
                b.wait("vector", sT, sT.n)
                b.op("vector", "tensor_copy",
                     out=Vt[:, 520 * tcnk:520 * (tcnk + 1)]
                     .rearrange("p (h c) -> p h c", c=65)[:, :, 0:64],
                     in_=dst.rearrange("p (h c) -> p h c", c=64), inc=sV)
            for kc in range(8):
                st_wv.free("tensor", toks_v[kc], sT)
            VT_V = sV.n

            for pl in range(4):
                p = 4 * g + pl
                if layer == 0:
                    tok_qk = st_qk.load([(0, 1024, dr["wqT"][p]), (1024, 1024, dr["wkT"][p])])
                else:
                    tok_qk = st_qk.load([(0, 1024, dr["wqk"][p]), (1024, 1024, dr["wqk"][8 + p])])
                st_qk.wait_use("tensor", tok_qk)
                if pl == 0:
                    b.wait("tensor", sV, max(VT_V, xn_marks[1]))
                else:
                    b.wait("tensor", sV, tailm[0][2])
                for which, dstT, colbase in ((0, Qp, 0), (1, Kp, 1024)):
                    for tq in range(4):
                        dst = q[2 * which + tq // 2][:, 512 * (tq % 2):512 * (tq % 2 + 1)]
                        for kc in range(8):
                            b.op("tensor", "matmul", out=dst,
                                 lhsT=st_qk.ap(tok_qk)[:, colbase + 128 * kc:colbase + 128 * (kc + 1)],
                                 rhs=XN[kc][:, 512 * tq:512 * (tq + 1)],
                                 start=(kc == 0), stop=(kc == 7 and layer == 0),
                                 inc=(sT if (kc == 7 and layer == 0 and tq == 3) else None))
                        if layer == 1:
                            b.op("tensor", "matmul", out=dst,
                                 lhsT=bqkrow[0:1, 1024 * which + 128 * p:1024 * which + 128 * (p + 1)],
                                 rhs=onesrow_b[0:1, :], start=False, stop=True,
                                 inc=(sT if tq == 3 else None))
                    b.wait("scalar", sT, sT.n)
                    for hh2 in range(2):
                        b.op("scalar", "activation", out=dstT[:, 1024 * hh2:1024 * (hh2 + 1)],
                             in_=q[2 * which + hh2][:, 0:1024], func=AF.Copy,
                             scale=(0.125 if which == 0 else 1.0),
                             inc=sS if hh2 == 1 else None)
                b.wait("tensor", sS, sS.n)
                st_qk.free("tensor", tok_qk, sT)
                b.op("scalar", "activation", out=SQ[:, 0:1024], in_=Qp[:, 0:1024], func=AF.Square)
                b.op("scalar", "activation", out=SQ[:, 1024:2048], in_=Qp[:, 1024:2048],
                     func=AF.Square, inc=sS)
                b.wait("tensor", sS, sS.n)
                for tq in range(4):
                    b.op("tensor", "matmul",
                         out=q[tq // 2][0:2, 512 * (tq % 2):512 * (tq % 2) + 512],
                         lhsT=bd2[:], rhs=SQ[:, 512 * tq:512 * (tq + 1)], start=True, stop=True,
                         inc=sT if tq == 3 else None)
                QN_T = sT.n
                b.wait("scalar", sT, QN_T)
                b.op("scalar", "activation", out=SQ[:, 0:1024], in_=Kp[:, 0:1024], func=AF.Square)
                b.op("scalar", "activation", out=SQ[:, 1024:2048], in_=Kp[:, 1024:2048],
                     func=AF.Square, inc=sS)
                b.wait("tensor", sS, sS.n)
                for tq in range(4):
                    b.op("tensor", "matmul",
                         out=q[2 + tq // 2][0:2, 512 * (tq % 2):512 * (tq % 2) + 512],
                         lhsT=bd2[:], rhs=SQ[:, 512 * tq:512 * (tq + 1)], start=True, stop=True,
                         inc=sT if tq == 3 else None)
                b.wait("vector", sT, QN_T)
                b.op("vector", "reduce_max", out=rowsA[0:2, 0:1], in_=q[0][0:2, :], axis=AX.X)
                b.op("vector", "reduce_max", out=rowsA[0:2, 1:2], in_=q[1][0:2, :], axis=AX.X,
                     inc=sV)
                REDQ = sV.n
                b.wait("vector", sT, sT.n)
                b.op("vector", "reduce_max", out=rowsA[0:2, 2:3], in_=q[2][0:2, :], axis=AX.X)
                b.op("vector", "reduce_max", out=rowsA[0:2, 3:4], in_=q[3][0:2, :], axis=AX.X,
                     inc=sV)
                REDK = sV.n
                b.op("vector", "tensor_tensor", out=rowsA[0:2, 4:5], in0=rowsA[0:2, 0:1],
                     in1=rowsA[0:2, 1:2], op=ALU.max)
                b.op("vector", "tensor_tensor", out=rowsA[0:2, 5:6], in0=rowsA[0:2, 2:3],
                     in1=rowsA[0:2, 3:4], op=ALU.max)
                b.op("vector", "tensor_tensor", out=rowsA[0:2, 6:7], in0=rowsA[0:2, 4:5],
                     in1=rowsA[0:2, 5:6], op=ALU.mult, inc=sV)
                b.wait("scalar", sV, sV.n)
                b.op("scalar", "activation", out=rowsA[0:2, 7:8], in_=rowsA[0:2, 6:7],
                     func=AF.Sqrt, inc=sS)
                SQRT_S = sS.n
                b.wait("vector", sS, sS.n)
                b.op("vector", "tensor_scalar", out=rowsA[0:2, 8:9], in0=rowsA[0:2, 7:8],
                     scalar1=-1.0, scalar2=None, op0=ALU.mult, inc=sV)
                NEGm = sV.n
                pairm = {"redq": REDQ, "redk": REDK, "negm": NEGm, "s1": SQRT_S,
                         "negb": 0}

                for hl in range(2):
                    hh = 2 * p + hl
                    hg = 2 * pl + hl
                    rr = slice(64 * hl, 64 * (hl + 1))
                    srow = 64 if hl == 0 else 32
                    segs = []
                    for i in range(16):
                        qstart = 128 * i
                        if qstart < 1024:
                            segs.append((i, qstart, 1024))
                        if max(qstart, 1024) < 2048:
                            segs.append((i, max(qstart, 1024), 2048))
                    exp_marks = [None] * len(segs)

                    def emit_scores_mm(sidx):
                        i, s0, s1 = segs[sidx]
                        dst = q[sidx % 2]
                        if sidx < 2:
                            b.wait("tensor", sV, max(tailm[0][sidx], pairm["redq"]))
                        else:
                            b.wait("tensor", sS, exp_marks[sidx - 2])
                        first_cut = (s0 // 512 + 1) * 512
                        cuts = [s0] + list(range(first_cut, s1, 512))
                        bounds = cuts + [s1]
                        for ci, c0 in enumerate(cuts):
                            c1 = bounds[ci + 1]
                            cc = c0 % 1024
                            b.op("tensor", "matmul", out=dst[:, cc:cc + (c1 - c0)],
                                 lhsT=Kp[rr, 128 * i:128 * (i + 1)],
                                 rhs=Qp[rr, c0:c1], start=True, stop=True,
                                 inc=(sT if ci == len(cuts) - 1 else None))
                        return (sidx, cuts, bounds, sT.n)

                    def emit_exp(mi, negb_wait=False):
                        sidx, cuts, bounds, tmark = mi
                        i, s0, s1 = segs[sidx]
                        dst = q[sidx % 2]
                        es = eslot[sidx % 2]
                        b.wait("scalar", sT, tmark)
                        if negb_wait:
                            b.wait("scalar", sV, pairm["negb"])
                        co0 = s0 % 1024
                        b.op("scalar", "activation", out=es[:, co0:co0 + (s1 - s0)],
                             in_=dst[:, co0:co0 + (s1 - s0)], func=AF.Exp,
                             bias=negB[:, hl:hl + 1], scale=1.0, inc=sS)
                        exp_marks[sidx] = sS.n
                        if s0 == 128 * i:
                            b.wait("vector", sS, sS.n)
                            b.op("vector", "tensor_tensor", out=es[:, co0:co0 + 128],
                                 in0=es[:, co0:co0 + 128], in1=trimask[:], op=ALU.mult,
                                 inc=sV)
                            return (sidx, cuts, bounds, sV.n, None)
                        return (sidx, cuts, bounds, None, sS.n)

                    def emit_av(pend):
                        sidx, cuts, bounds, vmark, smark = pend
                        i, s0, s1 = segs[sidx]
                        es = eslot[sidx % 2]
                        if sidx == 0:
                            b.wait("tensor", sV, max(tailm[0][2],
                                                     pairm["negb"] if hl == 0 else 0))
                        elif sidx == 1 and hl == 0:
                            b.wait("tensor", sV, pairm["redk"])
                        if vmark is not None:
                            b.wait("tensor", sV, vmark)
                        else:
                            b.wait("tensor", sS, smark)
                        for ci, c0 in enumerate(cuts):
                            c1 = bounds[ci + 1]
                            zt = q[2 + c0 // 1024]
                            zc = c0 % 1024
                            wv_ = c1 - c0
                            strt = (i == 0)
                            stp = (i == 4 * (c0 // 512) + 3)
                            fin = sT if stp else None
                            ein = es[:, (c0 % 1024):(c0 % 1024) + wv_]
                            vb = 520 * i + 65 * hg
                            if hl == 0:
                                b.op("tensor", "matmul", out=zt[0:64, zc:zc + wv_],
                                     lhsT=Vt[:, vb:vb + 64], rhs=ein, start=strt, stop=stp,
                                     skip_group_check=True)
                                b.op("tensor", "matmul", out=zt[64:65, zc:zc + wv_],
                                     lhsT=Vt[:, vb + 64:vb + 65], rhs=ein, start=strt,
                                     stop=stp, tile_position=(0, 64), skip_group_check=True,
                                     inc=fin)
                            else:
                                b.op("tensor", "matmul", out=zt[64:96, zc:zc + wv_],
                                     lhsT=Vt[:, vb:vb + 32], rhs=ein, start=strt, stop=stp,
                                     tile_position=(0, 64), skip_group_check=True)
                                b.op("tensor", "matmul", out=zt[96:128, zc:zc + wv_],
                                     lhsT=Vt[:, vb + 32:vb + 64], rhs=ein, start=strt,
                                     stop=stp, tile_position=(0, 96), skip_group_check=True)
                                b.op("tensor", "matmul", out=zt[32:33, zc:zc + wv_],
                                     lhsT=Vt[:, vb + 64:vb + 65], rhs=ein, start=strt,
                                     stop=stp, tile_position=(0, 32), skip_group_check=True,
                                     inc=fin)
                            if stp:
                                gq = c0 // 512
                                b.wait("vector", sT, sT.n)
                                b.op("vector", "reciprocal",
                                     out=rowsA[srow:srow + 1, 512 * gq:512 * (gq + 1)],
                                     in_=q[2 + gq // 2][srow:srow + 1,
                                                        512 * (gq % 2):512 * (gq % 2) + 512],
                                     inc=sV)

                    if hl == 0:
                        m0 = emit_scores_mm(0)
                        m1 = emit_scores_mm(1)
                        b.wait("tensor", sV, pairm["negm"])
                        b.op("tensor", "matmul", out=q[2][0:1, 0:2], lhsT=rowsA[0:2, 8:9],
                             rhs=ident8[0:2, 0:2], start=True, stop=True, inc=sT)
                        b.wait("vector", sT, sT.n)
                        b.op("vector", "tensor_copy", out=rows3[0:1, 0:2],
                             in_=q[2][0:1, 0:2], inc=sV)
                        b.wait("tensor", sV, sV.n)
                        b.op("tensor", "matmul", out=q[2][0:128, 2:4], lhsT=ones1[:],
                             rhs=rows3[0:1, 0:2], start=True, stop=True, inc=sT)
                        b.wait("vector", sT, sT.n)
                        b.wait("vector", sS, pairm["s1"])
                        b.op("vector", "tensor_copy", out=negB[:], in_=q[2][0:128, 2:4],
                             inc=sV)
                        pairm["negb"] = sV.n
                        p0 = emit_exp(m0, negb_wait=True)
                        pending = emit_exp(m1, negb_wait=True)
                        emit_av(p0)
                        start_sidx = 2
                    else:
                        pending = None
                        start_sidx = 0
                    for sidx in range(start_sidx, len(segs)):
                        mi = emit_scores_mm(sidx)
                        pend = emit_exp(mi)
                        if pending is not None:
                            emit_av(pending)
                        pending = pend
                    emit_av(pending)
                    AV_T = sT.n
                    b.wait("tensor", sV, sV.n)
                    for c in range(4):
                        b.op("tensor", "matmul",
                             out=q[c // 2][:, 512 * (c % 2):512 * (c % 2) + 512],
                             lhsT=onesP[srow:srow + 1, :],
                             rhs=rowsA[srow:srow + 1, 512 * c:512 * (c + 1)],
                             start=True, stop=True, inc=sT if c == 3 else None)
                    b.wait("vector", sT, sT.n)
                    zr = slice(0, 64) if hl == 0 else slice(64, 128)
                    qm = []
                    for c in range(2):
                        b.op("vector", "tensor_copy",
                             out=rowsA[zr, 1024 * c:1024 * (c + 1)], in_=q[c][zr, :],
                             inc=sV)
                        qm.append(sV.n)
                    for c in range(2):
                        if layer == 0:
                            b.op("vector", "tensor_tensor",
                                 out=zf[pl][zr, 1024 * c:1024 * (c + 1)],
                                 in0=q[2 + c][zr, :], in1=rowsA[zr, 1024 * c:1024 * (c + 1)],
                                 op=ALU.mult, inc=sV if c == 1 else None)
                        else:
                            b.op("vector", "tensor_tensor",
                                 out=rowsA[zr, 1024 * c:1024 * (c + 1)],
                                 in0=q[2 + c][zr, :], in1=rowsA[zr, 1024 * c:1024 * (c + 1)],
                                 op=ALU.mult, inc=sV if c == 1 else None)
                    tailm[0] = (qm[0], qm[1], sV.n)
                    if layer == 1:
                        for c in range(2):
                            b.op("vector", "tensor_tensor",
                                 out=X[p][zr, 1024 * c:1024 * (c + 1)],
                                 in0=X[p][zr, 1024 * c:1024 * (c + 1)],
                                 in1=rowsA[zr, 1024 * c:1024 * (c + 1)], op=ALU.add,
                                 inc=sV if c == 1 else None)

            if layer == 0:
                toks_wo = [st_wo.load([(0, 512, dr["woT2"][oc, g])])
                           for oc in range(2)]
                b.wait("tensor", sV, sV.n)
                for oc in range(8):
                    tok = toks_wo[oc % 2]
                    st_wo.wait_use("tensor", tok)
                    for tq in range(4):
                        dst = q[tq // 2][:, 512 * (tq % 2):512 * (tq % 2) + 512]
                        if oc > 0 and tq == 0:
                            b.wait("tensor", sV, sV.n)
                        for kcl in range(4):
                            b.op("tensor", "matmul", out=dst,
                                 lhsT=st_wo.ap(tok)[:, 128 * kcl:128 * (kcl + 1)],
                                 rhs=zf[kcl][:, 512 * tq:512 * (tq + 1)],
                                 start=(kcl == 0), stop=(kcl == 3),
                                 inc=(sT if (kcl == 3 and tq == 3) else None))
                    st_wo.free("tensor", tok, sT)
                    if oc < 6:
                        toks_wo[oc % 2] = st_wo.load(
                            [(0, 512, dr["woT2"][oc + 2, g])])
                    b.wait("vector", sT, sT.n)
                    for tq in range(4):
                        b.op("vector", "tensor_tensor",
                             out=X[oc][:, 512 * tq:512 * (tq + 1)],
                             in0=q[tq // 2][:, 512 * (tq % 2):512 * (tq % 2) + 512],
                             in1=X[oc][:, 512 * tq:512 * (tq + 1)], op=ALU.add,
                             inc=sV if tq == 3 else None)
                b.wait("tensor", sV, sV.n)
        b.pop()
        barrier()

    def tt_ffn(xn_marks):
        b.wait("tensor", sV, xn_marks[1])
        for oc in range(4):
            for out_im in range(2):
                gidx = 2 * oc + out_im
                pA = q[0] if gidx % 2 == 0 else q[2]
                pB = q[1] if gidx % 2 == 0 else q[3]
                if gidx >= 2:
                    b.wait("tensor", sV, sV.n - 1)
                W1 = WTre if out_im == 0 else WTim
                W2 = WTimn if out_im == 0 else WTre
                brow = brerow if out_im == 0 else bimrow
                for tq in range(4):
                    dst = (pA if tq < 2 else pB)[:, 512 * (tq % 2):512 * (tq % 2) + 512]
                    for kc in range(4):
                        b.op("tensor", "matmul", out=dst,
                             lhsT=W1[:, 512 * kc + 128 * oc:512 * kc + 128 * (oc + 1)],
                             rhs=XN[kc][:, 512 * tq:512 * (tq + 1)], start=(kc == 0), stop=False)
                    for kc in range(4):
                        b.op("tensor", "matmul", out=dst,
                             lhsT=W2[:, 512 * kc + 128 * oc:512 * kc + 128 * (oc + 1)],
                             rhs=XN[4 + kc][:, 512 * tq:512 * (tq + 1)], start=False, stop=False)
                    b.op("tensor", "matmul", out=dst,
                         lhsT=brow[0:1, 128 * oc:128 * (oc + 1)],
                         rhs=onesrow_b[0:1, :], start=False, stop=True,
                         inc=(sT if tq == 3 else None))
                b.wait("vector", sT, sT.n)
                xt = X[oc] if out_im == 0 else X[4 + oc]
                for tq in range(4):
                    b.op("vector", "scalar_tensor_tensor",
                         out=xt[:, 512 * tq:512 * (tq + 1)],
                         in0=(pA if tq < 2 else pB)[:, 512 * (tq % 2):512 * (tq % 2) + 512],
                         scalar=0.5, in1=xt[:, 512 * tq:512 * (tq + 1)],
                         op0=ALU.mult, op1=ALU.add, inc=sV if tq == 3 else None)

    def tap(name):
        if name in tap_out:
            barrier()
            b.wait("sync", sV, sV.n)
            b.wait("sync", sT, sT.n)
            for j in range(8):
                b.dma("sync", tap_out[name][128 * j:128 * (j + 1), :], X[j][:], inc=sST)

    build_tt()

    stages = [
        ("ffn0a", lambda: moe_ffn(layernorm())),
        ("attn0", lambda: attention(0, layernorm())),
        ("ffn0b", lambda: moe_ffn(layernorm())),
        ("ffn1a", lambda: tt_ffn(layernorm())),
        ("attn1", lambda: attention(1, layernorm())),
        ("ffn1b", lambda: tt_ffn(layernorm())),
    ]
    for name, fn in stages:
        fn()
        tap(name)
        if name == upto:
            break

    barrier()
    b.wait("sync", sV, sV.n)
    b.wait("sync", sT, sT.n)
    for j in range(8):
        b.dma("sync", y[128 * j:128 * (j + 1), :], X[j][:], inc=sST)
    b.wait("sync", sST, sST.n)
    b.wait("gpsimd", sST, sST.n)

    with nc.Block() as block:
        @block.sync
        def _(e):
            for f in b.ops["sync"]:
                f(e)

        @block.gpsimd
        def _(e):
            for f in b.ops["gpsimd"]:
                f(e)

        @block.tensor
        def _(e):
            for f in b.ops["tensor"]:
                f(e)

        @block.vector
        def _(e):
            for f in b.ops["vector"]:
                f(e)

        @block.scalar
        def _(e):
            for f in b.ops["scalar"]:
                f(e)

    b.close()
    return nc


def kernel(**inputs):
    inputs = {k: np.asarray(v) for k, v in inputs.items()}
    w = prep_host(inputs)
    xs = [np.ascontiguousarray(inputs["L0_attn_x"][c].T) for c in range(NCORES)]
    nc = build_program()
    in_maps = []
    for c in range(NCORES):
        m = dict(w)
        m["x"] = xs[c]
        in_maps.append(m)
    res = run_bass_kernel_spmd(nc, in_maps, core_ids=list(range(NCORES)))
    out = np.stack([np.asarray(r["y"]).T for r in res.results], axis=0).astype(np.float32)
    return out
